# version 04
# baseline (speedup 1.0000x reference)
import math
from contextlib import ExitStack

import numpy as np
import ml_dtypes
import concourse.bass as bass
import concourse.mybir as mybir
from concourse.bass_utils import run_bass_kernel_spmd

F32 = mybir.dt.float32
BF16 = mybir.dt.bfloat16
AF = mybir.ActivationFunctionType
ALU = mybir.AluOpType
AX = mybir.AxisListType

D = 2048
NCD = 16
NH = 16
TOK = 2048
NT = 16
ICF = 11
ICS = 22
NE = 16
NEG = -1e9
EPS = 1e-5
RSQ = 1.0 / math.sqrt(128.0)
BF = ml_dtypes.bfloat16
DEBUG = False
DBGV = 3
PROTO = 2


class Rec:

    STREAMS = ("sync", "gpsimd", "scalar", "vector", "tensor")

    def __init__(self, nc):
        self.nc = nc
        self.q = {s: [] for s in self.STREAMS}
        self.sem = {p: nc.alloc_semaphore(p) for p in
                    ("PE", "DVE", "ACT", "SY", "GY", "SD", "BOOT", "FIN")}
        self.cnt = {p: 0 for p in self.sem}
        self.mark = {}

    def wait(self, stream, prod, v):
        if not v or v <= 0:
            return
        k = (stream, prod)
        if self.mark.get(k, 0) >= v:
            return
        self.mark[k] = v
        sem = self.sem[prod]
        self.q[stream].append(lambda eng: eng.wait_ge(sem, v))

    def step(self, stream, prod, fn, n=1):
        sem = self.sem[prod]
        self.cnt[prod] += n
        self.q[stream].append(lambda eng: fn(eng).then_inc(sem, n))
        return self.cnt[prod]

    def stepw(self, stream, prod, fn, n=1):
        v = self.step(stream, prod, fn, n)
        self.wait(stream, prod, v)
        return v

    def barrier(self, streams, prods):
        for s in streams:
            for p in prods:
                self.wait(s, p, self.cnt[p])

    def run(self, fin_waits=(), proto=2):
        sems = self.sem
        with self.nc.Block() as block:
            def make(s):
                def body(eng):
                    if proto >= 1:
                        if s == "gpsimd":
                            for p in ("PE", "DVE", "ACT", "SY", "GY", "SD", "FIN"):
                                eng.sem_clear(sems[p])
                            eng.nop().then_inc(sems["BOOT"], 1)
                        else:
                            eng.wait_ge(sems["BOOT"], 1)
                    for f in self.q[s]:
                        f(eng)
                    if proto >= 2:
                        eng.nop().then_inc(sems["FIN"], 1)
                        if s == "gpsimd":
                            eng.wait_ge(sems["FIN"], 5)
                            eng.sem_clear(sems["BOOT"])
                return body
            block.sync(make("sync"))
            block.gpsimd(make("gpsimd"))
            block.scalar(make("scalar"))
            block.vector(make("vector"))
            block.tensor(make("tensor"))


class PsumRot:

    def __init__(self, rec, regions):
        self.rec = rec
        self.regions = regions
        self.free_ev = [None] * len(regions)
        self.i = 0

    def acquire(self, stream="tensor"):
        idx = self.i % len(self.regions)
        self.i += 1
        ev = self.free_ev[idx]
        if ev is not None:
            self.rec.wait(stream, ev[0], ev[1])
        return idx, self.regions[idx]

    def release(self, idx, prod, val):
        self.free_ev[idx] = (prod, val)


def build_program():
    nc = bass.Bass("TRN2", target_bir_lowering=False, debug=False)

    xo = nc.dram_tensor("xo", [D, TOK], F32, kind="ExternalInput").ap()
    xp = nc.dram_tensor("xp", [D, TOK], F32, kind="ExternalInput").ap()
    wq = nc.dram_tensor("wq", [48, 128, 2048], F32, kind="ExternalInput").ap()
    wo = nc.dram_tensor("wo", [16, 2, 128, 1024], F32, kind="ExternalInput").ap()
    gw = nc.dram_tensor("gw", [128, 256], F32, kind="ExternalInput").ap()
    w1 = nc.dram_tensor("w1", [NE, ICF, 128, 2048], BF16, kind="ExternalInput").ap()
    w3 = nc.dram_tensor("w3", [NE, ICF, 128, 2048], BF16, kind="ExternalInput").ap()
    w2 = nc.dram_tensor("w2", [NE, 16, 128, 1408], BF16, kind="ExternalInput").ap()
    sw1 = nc.dram_tensor("sw1", [ICS, 128, 2048], BF16, kind="ExternalInput").ap()
    sw3 = nc.dram_tensor("sw3", [ICS, 128, 2048], BF16, kind="ExternalInput").ap()
    sw2 = nc.dram_tensor("sw2", [16, 2, 128, ICF * 128], BF16, kind="ExternalInput").ap()
    cblobb = nc.dram_tensor("cblobb", [128, 1152], BF16, kind="ExternalInput").ap()
    cblobf = nc.dram_tensor("cblobf", [128, 576], F32, kind="ExternalInput").ap()
    seld = nc.dram_tensor("sel", [16, 2048], BF16, kind="ExternalInput").ap()
    onesrd = nc.dram_tensor("onesr", [1, 128], F32, kind="ExternalInput").ap()
    fpad = nc.dram_tensor("fpad", [128, 16], F32, kind="ExternalInput").ap()
    yT = nc.dram_tensor("yT", [D, TOK], F32, kind="ExternalOutput").ap()
    osc = nc.dram_tensor("osc", [NH, 128, TOK], BF16,
                         kind="ExternalOutput" if DEBUG else "Internal").ap()
    if DEBUG:
        d_xn = nc.dram_tensor("d_xn", [2, 128, NCD * TOK], BF16, kind="ExternalOutput").ap()
        d_gate = nc.dram_tensor("d_gate", [128, 256], F32, kind="ExternalOutput").ap()
        d_dwem = nc.dram_tensor("d_dwem", [16, 2048], BF16, kind="ExternalOutput").ap()
        d_qkv = nc.dram_tensor("d_qkv", [3, 128, 4096], BF16, kind="ExternalOutput").ap()
        d_dwtm = nc.dram_tensor("d_dwtm", [128, 256], F32, kind="ExternalOutput").ap()
        d_rows = nc.dram_tensor("d_rows", [2, 128, 4096], F32, kind="ExternalOutput").ap()
        d_rowb = nc.dram_tensor("d_rowb", [2, 1, 4096], F32, kind="ExternalOutput").ap()
        d_ysb = nc.dram_tensor("d_ysb", [2, 128, 1024], F32, kind="ExternalOutput").ap()
        d_stage = nc.dram_tensor("d_stage", [2, 8, 128, 1024], F32, kind="ExternalOutput").ap()
        d_scr = nc.dram_tensor("d_scr", [2, 2, 128, 1024], F32, kind="ExternalOutput").ap()

    xn_own = nc.alloc_sbuf_tensor("xn_own", [128, NCD * TOK], BF16)
    s_cbb = nc.alloc_sbuf_tensor("s_cbb", [128, 1152], BF16)
    s_cbf = nc.alloc_sbuf_tensor("s_cbf", [128, 576], F32)
    s_ident = s_cbb[:, 0:128]
    s_dmask = s_cbb[:, 128:256]
    s_cmask = s_cbb[:, 256:384]
    s_onesc = s_cbb[:, 384:388]
    s_bq = s_cbf[:, 0:48]
    s_bo = s_cbf[:, 48:64]
    s_bqs = nc.alloc_sbuf_tensor("s_bqs", [128, 16], F32)
    s_gw = nc.alloc_sbuf_tensor("s_gw", [128, 256], BF16)
    s_sel = nc.alloc_sbuf_tensor("s_sel", [16, 2048], BF16)
    s_onesr = nc.alloc_sbuf_tensor("s_onesr", [1, 128], F32)
    s_gate = nc.alloc_sbuf_tensor("s_gate", [128, 256], F32)
    dw_tm = nc.alloc_sbuf_tensor("dw_tm", [128, 256], F32)
    dw_tmb = nc.alloc_sbuf_tensor("dw_tmb", [128, 256], BF16)
    dw_em = nc.alloc_sbuf_tensor("dw_em", [16, 2048], BF16)
    nm_cols = [nc.alloc_sbuf_tensor(f"nm_cols{i}", [128, 8], F32) for i in range(2)]
    l_cols = [nc.alloc_sbuf_tensor(f"l_cols{i}", [128, 8], F32) for i in range(2)]
    r_cols = [nc.alloc_sbuf_tensor(f"r_cols{i}", [128, 8], F32) for i in range(2)]
    nm_j = nc.alloc_sbuf_tensor("nm_j", [128, 1], F32)
    l_j = nc.alloc_sbuf_tensor("l_j", [128, 1], F32)
    rinv_j = nc.alloc_sbuf_tensor("rinv_j", [128, 1], F32)
    tp2 = nc.alloc_sbuf_tensor("tp2", [128, 20], F32)
    s_fpad = nc.alloc_sbuf_tensor("s_fpad", [128, 16], F32)

    rec = Rec(nc)
    W = rec.wait
    S = rec.step

    def flush(queue):
        sem = "SY" if queue == "sync" else "GY"
        return S(queue, sem, lambda eng: eng.dma_start(out=s_fpad[:], in_=fpad), 16)

    ev_const = 0
    for dst, src in ((s_cbb, cblobb), (s_cbf, cblobf),
                     (s_sel, seld), (s_onesr, onesrd)):
        ev_const = S("sync", "SY",
                     lambda eng, d=dst, s=src: eng.dma_start(out=d[:], in_=s), 16)
    flush("sync")
    ev_gw = S("gpsimd", "GY", lambda eng: eng.dma_start(out=s_gw[:], in_=gw), 16)
    W("vector", "SY", ev_const + 16)
    ev_bqs = S("vector", "DVE", lambda eng: eng.tensor_scalar_mul(
        s_bqs[:], s_bq[:, 0:16], RSQ))

    pxn = ExitStack()
    xn_prt = pxn.enter_context(nc.sbuf_tensor("xn_prt", [128, NCD * TOK], BF16))
    p1 = ExitStack()
    rowA = p1.enter_context(nc.sbuf_tensor("rowA", [1, 2048], F32))
    rowB = p1.enter_context(nc.sbuf_tensor("rowB", [1, 2048], F32))
    rowT = p1.enter_context(nc.sbuf_tensor("rowT", [1, 2048], F32))
    rstd_b = p1.enter_context(nc.sbuf_tensor("rstd_b", [128, 2048], F32))
    mr_b = p1.enter_context(nc.sbuf_tensor("mr_b", [128, 2048], F32))
    x2st = [p1.enter_context(nc.sbuf_tensor(f"x2st{i}", [128, 1024], BF16))
            for i in range(2)]
    ps1 = ExitStack()
    ps_sx = ps1.enter_context(nc.psum_tensor("ps_sx", [4, 1024], F32))
    ps_sx2 = ps1.enter_context(nc.psum_tensor("ps_sx2", [4, 1024], F32))
    ps_bc = ps1.enter_context(nc.psum_tensor("ps_bc", [128, 1024], F32))

    ev_xld = {}
    for half, (xsrc, xbuf) in (("own", (xo, xn_own)), ("prt", (xp, xn_prt))):
        for c in range(NCD):
            ev_xld[(half, c)] = S(
                "gpsimd", "GY", lambda eng, c=c, xs=xsrc, xb=xbuf: eng.dma_start(
                    out=xb[:, c * TOK:(c + 1) * TOK],
                    in_=xs[c * 128:(c + 1) * 128, :]), 16)
    flush("gpsimd")

    ev_bc_evac = None
    ev_sq_mm = []
    for half, xbuf in (("own", xn_own), ("prt", xn_prt)):
        for th in range(2):
            for c in range(NCD):
                W("scalar", "GY", ev_xld[(half, c)] + 16)
                if len(ev_sq_mm) >= 2:
                    W("scalar", "PE", ev_sq_mm[-2])
                ev_sq = S("scalar", "ACT", lambda eng, c=c, th=th, xb=xbuf: eng.activation(
                    x2st[c % 2][:], xb[:, c * TOK + th * 1024: c * TOK + th * 1024 + 1024],
                    AF.Square))
                W("tensor", "GY", ev_xld[(half, c)] + 16)
                W("tensor", "SY", ev_const + 16)
                if half == "prt" and th == 0 and c == 0 and ev_bc_evac is not None:
                    W("tensor", "DVE", ev_bc_evac)
                for t in range(2):
                    S("tensor", "PE", lambda eng, c=c, t=t, th=th, xb=xbuf: eng.matmul(
                        ps_sx[:, t * 512:(t + 1) * 512], s_onesc,
                        xb[:, c * TOK + th * 1024 + t * 512: c * TOK + th * 1024 + t * 512 + 512],
                        start=(c == 0), stop=(c == NCD - 1)))
                W("tensor", "ACT", ev_sq)
                for t in range(2):
                    ev = S("tensor", "PE", lambda eng, c=c, t=t: eng.matmul(
                        ps_sx2[:, t * 512:(t + 1) * 512], s_onesc,
                        x2st[c % 2][:, t * 512:(t + 1) * 512],
                        start=(c == 0), stop=(c == NCD - 1)))
                ev_sq_mm.append(ev)
            W("vector", "PE", rec.cnt["PE"])
            S("vector", "DVE", lambda eng, th=th: eng.tensor_scalar_mul(
                rowA[:, th * 1024:(th + 1) * 1024], ps_sx[0:1, :], 1.0 / 2048.0))
            ev_rows = S("vector", "DVE", lambda eng, th=th: eng.tensor_scalar_mul(
                rowB[:, th * 1024:(th + 1) * 1024], ps_sx2[0:1, :], 1.0 / 2048.0))
            W("tensor", "DVE", ev_rows)
        S("vector", "DVE", lambda eng: eng.tensor_mul(rowT[:], rowA[:], rowA[:]))
        ev_var = S("vector", "DVE", lambda eng: eng.tensor_sub(rowB[:], rowB[:], rowT[:]))
        ev_var = S("vector", "DVE", lambda eng: eng.tensor_scalar_add(rowB[:], rowB[:], EPS))
        W("scalar", "DVE", ev_var)
        S("scalar", "ACT", lambda eng: eng.activation(rowB[:], rowB[:], AF.Ln))
        ev_rstd = S("scalar", "ACT", lambda eng: eng.activation(rowB[:], rowB[:], AF.Exp, scale=-0.5))
        W("tensor", "ACT", ev_rstd)
        for t in range(4):
            half_sl = (t % 2) * 512
            ev = S("tensor", "PE", lambda eng, t=t, half_sl=half_sl: eng.matmul(
                ps_bc[:, half_sl:half_sl + 512], s_onesr[:],
                rowB[0:1, t * 512:(t + 1) * 512], start=True, stop=True))
            W("vector", "PE", ev)
            ev2 = S("vector", "DVE", lambda eng, t=t, half_sl=half_sl: eng.tensor_copy(
                rstd_b[:, t * 512:(t + 1) * 512], ps_bc[:, half_sl:half_sl + 512]))
            W("tensor", "DVE", ev2)
        for t in range(4):
            half_sl = (t % 2) * 512
            ev = S("tensor", "PE", lambda eng, t=t, half_sl=half_sl: eng.matmul(
                ps_bc[:, half_sl:half_sl + 512], s_onesr[:],
                rowA[0:1, t * 512:(t + 1) * 512], start=True, stop=True))
            W("vector", "PE", ev)
            ev2 = S("vector", "DVE", lambda eng, t=t, half_sl=half_sl: eng.tensor_copy(
                mr_b[:, t * 512:(t + 1) * 512], ps_bc[:, half_sl:half_sl + 512]))
            ev_bc_evac = ev2
            W("tensor", "DVE", ev2)
        ev_mrb = S("vector", "DVE", lambda eng: eng.tensor_mul(mr_b[:], mr_b[:], rstd_b[:]))
        if DEBUG:
            hi = 0 if half == "own" else 1
            W("gpsimd", "DVE", ev_mrb)
            S("gpsimd", "GY", lambda eng, hi=hi: eng.dma_start(
                out=d_rows[hi][:, 0:2048], in_=rstd_b[:]), 16)
            S("gpsimd", "GY", lambda eng, hi=hi: eng.dma_start(
                out=d_rows[hi][:, 2048:4096], in_=mr_b[:]), 16)
            S("gpsimd", "GY", lambda eng, hi=hi: eng.dma_start(
                out=d_rowb[hi][:, 0:2048], in_=rowA[:]), 16)
            ev_dbr = S("gpsimd", "GY", lambda eng, hi=hi: eng.dma_start(
                out=d_rowb[hi][:, 2048:4096], in_=rowB[:]), 16)
            W("vector", "GY", ev_dbr)
            W("tensor", "GY", ev_dbr)
        for c in range(NCD):
            S("vector", "DVE", lambda eng, c=c, xb=xbuf: eng.tensor_mul(
                xb[:, c * TOK:(c + 1) * TOK], xb[:, c * TOK:(c + 1) * TOK], rstd_b[:]))
            S("vector", "DVE", lambda eng, c=c, xb=xbuf: eng.tensor_sub(
                xb[:, c * TOK:(c + 1) * TOK], xb[:, c * TOK:(c + 1) * TOK], mr_b[:]))
    ev_xn_all = rec.cnt["DVE"]
    rec.barrier(("tensor", "scalar", "gpsimd", "sync"), ("DVE",))
    p1.close()
    ps1.close()

    pg = ExitStack()
    ps_g = [pg.enter_context(nc.psum_tensor(f"ps_g{i}", [128, 16], F32)) for i in range(2)]
    ps_dwT = pg.enter_context(nc.psum_tensor("ps_dwT", [16, 128], BF16))
    W("tensor", "GY", ev_gw + 16)
    W("tensor", "DVE", ev_xn_all)
    ev_sig = [None] * NT
    ev_top2 = [None] * NT
    for tau in range(NT):
        pg_i = tau % 2
        if tau >= 2:
            W("tensor", "DVE", ev_top2[tau - 2])
        for c in range(NCD):
            ev = S("tensor", "PE", lambda eng, tau=tau, c=c, pg_i=pg_i: eng.matmul(
                ps_g[pg_i][:], xn_own[:, c * TOK + tau * 128: c * TOK + tau * 128 + 128],
                s_gw[:, c * 16:(c + 1) * 16], start=(c == 0), stop=(c == NCD - 1)))
        W("scalar", "PE", ev)
        ev_sig[tau] = S("scalar", "ACT", lambda eng, tau=tau, pg_i=pg_i: eng.activation(
            s_gate[:, tau * 16:(tau + 1) * 16], ps_g[pg_i][:], AF.Sigmoid))
        W("vector", "ACT", ev_sig[tau])
        SW = rec.stepw
        sg = s_gate[:, tau * 16:(tau + 1) * 16]
        eq = tp2[:, 0:16]
        m1c, m2c, denc, rdenc = (tp2[:, 16:17], tp2[:, 17:18],
                                 tp2[:, 18:19], tp2[:, 19:20])
        SW("vector", "DVE", lambda eng, sg=sg, m1c=m1c: eng.reduce_max(m1c, sg, axis=AX.X))
        SW("vector", "DVE", lambda eng, sg=sg, eq=eq, m1c=m1c: eng.tensor_scalar(
            eq, sg, m1c, None, op0=ALU.subtract))
        SW("vector", "DVE", lambda eng, eq=eq: eng.tensor_single_scalar(
            eq, eq, 0.0, op=ALU.is_equal))
        SW("vector", "DVE", lambda eng, sg=sg, eq=eq: eng.scalar_tensor_tensor(
            eq, eq, -1e30, sg, op0=ALU.mult, op1=ALU.add))
        SW("vector", "DVE", lambda eng, eq=eq, m2c=m2c: eng.reduce_max(m2c, eq, axis=AX.X))
        SW("vector", "DVE", lambda eng, m1c=m1c, m2c=m2c, denc=denc: eng.tensor_add(denc, m1c, m2c))
        SW("vector", "DVE", lambda eng, denc=denc, rdenc=rdenc: eng.reciprocal(rdenc, denc))
        SW("vector", "DVE", lambda eng, sg=sg, eq=eq, m2c=m2c: eng.tensor_scalar(
            eq, sg, m2c, None, op0=ALU.subtract))
        SW("vector", "DVE", lambda eng, eq=eq: eng.tensor_single_scalar(
            eq, eq, 0.0, op=ALU.is_ge))
        SW("vector", "DVE", lambda eng, sg=sg, eq=eq, rdenc=rdenc, tau=tau: eng.scalar_tensor_tensor(
            dw_tm[:, tau * 16:(tau + 1) * 16], sg, rdenc, eq, op0=ALU.mult, op1=ALU.mult))
        ev_dw = SW("vector", "DVE", lambda eng, tau=tau: eng.tensor_copy(
            dw_tmb[:, tau * 16:(tau + 1) * 16], dw_tm[:, tau * 16:(tau + 1) * 16]))
        ev_top2[tau] = ev_dw
        W("tensor", "DVE", ev_dw)
        evt = S("tensor", "PE", lambda eng, tau=tau: eng.transpose(
            ps_dwT[:], dw_tmb[:, tau * 16:(tau + 1) * 16], s_ident))
        W("vector", "PE", evt)
        ev_dwem = S("vector", "DVE", lambda eng, tau=tau: eng.tensor_copy(
            dw_em[:, tau * 128:(tau + 1) * 128], ps_dwT[:]))
        W("tensor", "DVE", ev_dwem)
    ev_dw_done = rec.cnt["DVE"]
    if DEBUG:
        W("gpsimd", "DVE", rec.cnt["DVE"])
        W("gpsimd", "ACT", rec.cnt["ACT"])
        S("gpsimd", "GY", lambda eng: eng.dma_start(out=d_xn[0], in_=xn_own[:]), 16)
        S("gpsimd", "GY", lambda eng: eng.dma_start(out=d_xn[1], in_=xn_prt[:]), 16)
        S("gpsimd", "GY", lambda eng: eng.dma_start(out=d_gate, in_=s_gate[:]), 16)
        S("gpsimd", "GY", lambda eng: eng.dma_start(out=d_dwem, in_=dw_em[:]), 16)
        S("gpsimd", "GY", lambda eng: eng.dma_start(out=d_dwtm, in_=dw_tm[:]), 16)
    rec.barrier(("tensor", "sync", "gpsimd"), ("DVE", "ACT"))
    pg.close()

    pa = ExitStack()
    s_wring = [pa.enter_context(nc.sbuf_tensor(f"wring{i}", [128, 2048], BF16))
               for i in range(4)]
    s_q = pa.enter_context(nc.sbuf_tensor("s_q", [128, 2048], BF16))
    s_k = pa.enter_context(nc.sbuf_tensor("s_k", [128, 4096], BF16))
    s_vf = pa.enter_context(nc.sbuf_tensor("s_vf", [128, 512], BF16))
    s_vT = pa.enter_context(nc.sbuf_tensor("s_vT", [128, 4096], BF16))
    s_probs = [pa.enter_context(nc.sbuf_tensor(f"probs{i}", [128, 4096], BF16))
               for i in range(2)]
    s_pt = pa.enter_context(nc.sbuf_tensor("s_pt", [128, 2048], BF16))
    s_ost = pa.enter_context(nc.sbuf_tensor("s_ost", [128, 2048], BF16))
    s_obf = pa.enter_context(nc.sbuf_tensor("s_obf", [128, 128], BF16))
    pap = ExitStack()
    ps_sc = [pap.enter_context(nc.psum_tensor(f"ps_sc{i}", [128, 512], F32)) for i in range(4)]
    ps_pt = [pap.enter_context(nc.psum_tensor(f"ps_pt{i}", [128, 1024], BF16)) for i in range(2)]
    ps_oa = pap.enter_context(nc.psum_tensor("ps_oa", [128, 512], F32))

    sc_rot = PsumRot(rec, ps_sc)
    pt_rot = PsumRot(rec, ps_pt)
    oa_free = [None]
    pt_sb_parity = [0]
    ev_pt_read = [None, None]

    strip_ev = {}
    slot_of = {}
    slot_free = [None] * 4
    strip_seq = [(h, w) for h in range(NH) for w in range(3)]
    strip_i = [0]

    def emit_strip_loads(n):
        for _ in range(n):
            if strip_i[0] >= len(strip_seq):
                return
            h, w = strip_seq[strip_i[0]]
            sl = strip_i[0] % 4
            strip_i[0] += 1
            if slot_free[sl] is not None:
                W("gpsimd", "PE", slot_free[sl])
            co = (w * 16) + h
            strip_ev[(h, w)] = S("gpsimd", "GY", lambda eng, sl=sl, co=co: eng.dma_start(
                out=s_wring[sl][:], in_=wq[co]), 16)
            slot_of[(h, w)] = sl

    emit_strip_loads(4)
    W("scalar", "DVE", ev_bqs)
    W("scalar", "SY", ev_const)
    ev_o_spill = [None]
    ev_q_lastread = [None]
    ev_k_lastread = [None]
    ev_v_lastread = [None]
    ev_probs_freed = [None, None]
    ev_vf_read = [None]

    for h in range(NH):
        for w in range(3):
            nchunks = 4 if w == 0 else 8
            sl = slot_of[(h, w)]
            for t in range(nchunks):
                ridx, reg = sc_rot.acquire()
                W("tensor", "GY", strip_ev[(h, w)] + 16)
                xbuf = xn_own if t < 4 else xn_prt
                toff = (t % 4) * 512
                for c in range(NCD):
                    ev_pe_proj = S("tensor", "PE", lambda eng, sl=sl, c=c, xbuf=xbuf,
                                   toff=toff, reg=reg: eng.matmul(
                        reg[:, 0:512], s_wring[sl][:, c * 128:(c + 1) * 128],
                        xbuf[:, c * TOK + toff: c * TOK + toff + 512],
                        start=(c == 0), stop=(c == NCD - 1)))
                W("scalar", "PE", ev_pe_proj)
                if w == 0:
                    if ev_q_lastread[0] is not None:
                        W("scalar", "PE", ev_q_lastread[0])
                    eva = S("scalar", "ACT", lambda eng, t=t, reg=reg, h=h: eng.activation(
                        s_q[:, t * 512:(t + 1) * 512], reg[:, 0:512], AF.Identity,
                        bias=s_bqs[:, h:h + 1], scale=RSQ))
                elif w == 1:
                    if ev_k_lastread[0] is not None:
                        W("scalar", "PE", ev_k_lastread[0])
                    eva = S("scalar", "ACT", lambda eng, t=t, reg=reg, h=h: eng.activation(
                        s_k[:, t * 512:(t + 1) * 512], reg[:, 0:512], AF.Identity,
                        bias=s_bq[:, 16 + h:17 + h]))
                else:
                    if ev_vf_read[0] is not None:
                        W("scalar", "PE", ev_vf_read[0])
                    eva = S("scalar", "ACT", lambda eng, reg=reg, h=h: eng.activation(
                        s_vf[:], reg[:, 0:512], AF.Identity,
                        bias=s_bq[:, 32 + h:33 + h]))
                sc_rot.release(ridx, "ACT", eva)
                if w == 2:
                    pidx, preg = pt_rot.acquire()
                    W("tensor", "ACT", eva)
                    for i in range(4):
                        evt = S("tensor", "PE", lambda eng, i=i, preg=preg: eng.transpose(
                            preg[:, i * 128:(i + 1) * 128], s_vf[:, i * 128:(i + 1) * 128],
                            s_ident))
                    ev_vf_read[0] = evt
                    W("vector", "PE", evt)
                    if ev_v_lastread[0] is not None:
                        W("vector", "PE", ev_v_lastread[0])
                    evv = S("vector", "DVE", lambda eng, t=t, preg=preg: eng.tensor_copy(
                        s_vT[:, t * 512:(t + 1) * 512], preg[:, 0:512]))
                    pt_rot.release(pidx, "DVE", evv)
            slot_free[sl] = rec.cnt["PE"]
            emit_strip_loads(1)
        ev_v_ready = rec.cnt["DVE"]
        ev_k_ready = rec.cnt["ACT"]
        if DEBUG and h == 0:
            W("gpsimd", "DVE", ev_v_ready)
            W("gpsimd", "ACT", ev_k_ready)
            S("gpsimd", "GY", lambda eng: eng.dma_start(out=d_qkv[0][:, 0:2048], in_=s_q[:]), 16)
            S("gpsimd", "GY", lambda eng: eng.dma_start(out=d_qkv[1], in_=s_k[:]), 16)
            ev_dbgq = S("gpsimd", "GY", lambda eng: eng.dma_start(out=d_qkv[2], in_=s_vT[:]), 16)
            W("scalar", "GY", ev_dbgq)
            W("vector", "GY", ev_dbgq)

        def emit_scores(j):
            pb = j % 2
            if ev_probs_freed[pb] is not None:
                W("scalar", "PE", ev_probs_freed[pb])
            nkt = j + 1
            chunks = []
            coff = 0
            ci = 0
            for sbase in (0, 2048):
                done = 0
                while done < nkt:
                    take = min(4, nkt - done)
                    clen = take * 128
                    ridx, reg = sc_rot.acquire()
                    W("tensor", "ACT", ev_k_ready)
                    plan = []
                    off = 0
                    while off < clen:
                        w512 = min(512, clen - off)
                        kt0 = done + off // 128
                        ktn = kt0 + w512 // 128
                        msk = None
                        if kt0 <= j < ktn:
                            msk = (off + (j - kt0) * 128,
                                   s_dmask if sbase == 0 else s_cmask)
                        plan.append((off, w512, msk))
                        off += w512
                    ev_pe_chunk = None
                    for pi, (off, w512, msk) in enumerate(plan):
                        last = pi == len(plan) - 1
                        def mm(eng, reg=reg, off=off, w512=w512, sbase=sbase,
                               done=done, j=j, msk=msk):
                            return eng.matmul(
                                reg[:, off:off + w512], s_q[:, j * 128:(j + 1) * 128],
                                s_k[:, sbase + done * 128 + off: sbase + done * 128 + off + w512],
                                start=True, stop=msk is None)
                        if msk is None and last:
                            ev_pe_chunk = S("tensor", "PE", mm)
                        else:
                            rec.q["tensor"].append(lambda eng, mm=mm: mm(eng))
                        if msk is not None:
                            def mmm(eng, reg=reg, msk=msk):
                                return eng.matmul(
                                    reg[:, msk[0]:msk[0] + 128], s_ident, msk[1][:],
                                    start=False, stop=True)
                            if last:
                                ev_pe_chunk = S("tensor", "PE", mmm)
                            else:
                                rec.q["tensor"].append(lambda eng, mmm=mmm: mmm(eng))
                    W("vector", "PE", ev_pe_chunk)
                    evm = S("vector", "DVE", lambda eng, reg=reg, clen=clen, ci=ci, pb=pb:
                            eng.reduce_max(nm_cols[pb][:, ci:ci + 1], reg[:, 0:clen],
                                           axis=AX.X, negate=True))
                    W("scalar", "DVE", evm)
                    eve = S("scalar", "ACT", lambda eng, reg=reg, clen=clen, ci=ci,
                            pb=pb, coff=coff: eng.activation(
                        s_probs[pb][:, coff:coff + clen], reg[:, 0:clen], AF.Exp,
                        bias=nm_cols[pb][:, ci:ci + 1], accum_out=l_cols[pb][:, ci:ci + 1]))
                    sc_rot.release(ridx, "ACT", eve)
                    chunks.append((coff, clen, ci, eve))
                    coff += clen
                    ci += 1
                    done += take
            return pb, chunks, ci

        def emit_softmax_fix(j, pb, chunks, nch):
            W("vector", "ACT", chunks[-1][3])
            W("vector", "DVE", rec.cnt["DVE"])
            evn = rec.stepw("vector", "DVE", lambda eng, nch=nch, pb=pb: eng.tensor_reduce(
                nm_j[:], nm_cols[pb][:, 0:nch], axis=AX.X, op=ALU.min))
            W("scalar", "DVE", evn)
            evr = S("scalar", "ACT", lambda eng, nch=nch, pb=pb: eng.activation(
                r_cols[pb][:, 0:nch], nm_cols[pb][:, 0:nch], AF.Exp,
                bias=nm_j[:], scale=-1.0))
            W("vector", "ACT", evr)
            rec.stepw("vector", "DVE", lambda eng, nch=nch, pb=pb: eng.tensor_mul(
                l_cols[pb][:, 0:nch], l_cols[pb][:, 0:nch], r_cols[pb][:, 0:nch]))
            rec.stepw("vector", "DVE", lambda eng, nch=nch, pb=pb: eng.reduce_sum(
                l_j[:], l_cols[pb][:, 0:nch], axis=AX.X))
            rec.stepw("vector", "DVE", lambda eng: eng.reciprocal(rinv_j[:], l_j[:]))
            ev_res = rec.cnt["DVE"]
            for coff, clen, ci, _eve in chunks:
                ev_res = S("vector", "DVE", lambda eng, coff=coff, clen=clen, ci=ci, pb=pb:
                           eng.tensor_scalar_mul(s_probs[pb][:, coff:coff + clen],
                                                 s_probs[pb][:, coff:coff + clen],
                                                 r_cols[pb][:, ci:ci + 1]))
            return ev_res

        def emit_av(j, pb, ev_rescale):
            nkt2 = 2 * (j + 1)

            def vt_tile(i):
                return i if i <= j else 16 + (i - (j + 1))
            if oa_free[0] is not None:
                W("tensor", oa_free[0][0], oa_free[0][1])
            W("tensor", "DVE", ev_v_ready)
            bi = 0
            ev_av = None
            while bi < nkt2:
                take = min(8, nkt2 - bi)
                pidx, preg = pt_rot.acquire()
                W("tensor", "DVE", ev_rescale)
                for i in range(take):
                    evt = S("tensor", "PE", lambda eng, preg=preg, i=i, bi=bi, pb=pb: eng.transpose(
                        preg[:, i * 128:(i + 1) * 128],
                        s_probs[pb][:, (bi + i) * 128:(bi + i + 1) * 128], s_ident))
                W("vector", "PE", evt)
                par = pt_sb_parity[0]
                pt_sb_parity[0] ^= 1
                if ev_pt_read[par] is not None:
                    W("vector", "PE", ev_pt_read[par])
                evv = S("vector", "DVE", lambda eng, preg=preg, take=take, par=par: eng.tensor_copy(
                    s_pt[:, par * 1024: par * 1024 + take * 128], preg[:, 0:take * 128]))
                pt_rot.release(pidx, "DVE", evv)
                W("tensor", "DVE", evv)
                for i in range(take):
                    kt = bi + i
                    vt = vt_tile(kt)
                    ev_av = S("tensor", "PE", lambda eng, i=i, vt=vt, kt=kt, nkt2=nkt2, par=par: eng.matmul(
                        ps_oa[:, 0:128],
                        s_pt[:, par * 1024 + i * 128: par * 1024 + (i + 1) * 128],
                        s_vT[:, vt * 128:(vt + 1) * 128],
                        start=(kt == 0), stop=(kt == nkt2 - 1)))
                ev_pt_read[par] = ev_av
                bi += take
            ev_probs_freed[pb] = rec.cnt["PE"]
            W("vector", "PE", ev_av)
            ev_sc = S("vector", "DVE", lambda eng: eng.tensor_scalar_mul(
                s_obf[:], ps_oa[:, 0:128], rinv_j[:]))
            oa_free[0] = ("DVE", ev_sc)
            pidx, preg = pt_rot.acquire()
            W("tensor", "DVE", ev_sc)
            evt = S("tensor", "PE", lambda eng, preg=preg: eng.transpose(
                preg[:, 0:128], s_obf[:], s_ident))
            W("vector", "PE", evt)
            if ev_o_spill[0] is not None:
                W("vector", "GY", ev_o_spill[0])
            evo = S("vector", "DVE", lambda eng, j=j, preg=preg: eng.tensor_copy(
                s_ost[:, j * 128:(j + 1) * 128], preg[:, 0:128]))
            pt_rot.release(pidx, "DVE", evo)
            return evo

        pend = None
        ev_ost_last = None
        for j in range(NT):
            sc = emit_scores(j)
            if pend is not None:
                jj, ppb, cchunks, nnch = pend
                ev_res = emit_softmax_fix(jj, ppb, cchunks, nnch)
                ev_ost_last = emit_av(jj, ppb, ev_res)
            pend = (j, sc[0], sc[1], sc[2])
        jj, ppb, cchunks, nnch = pend
        ev_res = emit_softmax_fix(jj, ppb, cchunks, nnch)
        ev_ost_last = emit_av(jj, ppb, ev_res)

        ev_q_lastread[0] = rec.cnt["PE"]
        ev_k_lastread[0] = rec.cnt["PE"]
        ev_v_lastread[0] = rec.cnt["PE"]
        W("gpsimd", "DVE", ev_ost_last)
        ev_o_spill[0] = S("gpsimd", "GY", lambda eng, h=h: eng.dma_start(
            out=osc[h], in_=s_ost[:]), 16)

    ev_o_spill_all = rec.cnt["GY"]
    rec.barrier(("tensor", "sync", "gpsimd", "scalar", "vector"), ("PE", "DVE", "ACT"))
    pa.close()
    pap.close()
    pxn.close()

    pf = ExitStack()
    y_sb = pf.enter_context(nc.sbuf_tensor("y_sb", [128, 16 * 1024], F32))
    scr = [pf.enter_context(nc.sbuf_tensor(f"scr{i}", [128, 1024], F32)) for i in range(2)]
    dwb = [pf.enter_context(nc.sbuf_tensor(f"dwb{i}", [128, 1024], F32)) for i in range(2)]
    fring = [pf.enter_context(nc.sbuf_tensor(f"fring{i}", [128, 2048], BF16)) for i in range(6)]
    pfp = ExitStack()
    ps_ab = [pfp.enter_context(nc.psum_tensor(f"ps_ab{i}", [128, 512], F32)) for i in range(4)]
    ps_y = [pfp.enter_context(nc.psum_tensor(f"ps_y{i}", [128, 512], F32)) for i in range(2)]
    ps_dwb = pfp.enter_context(nc.psum_tensor("ps_dwb", [128, 1024], F32))
    ab_rot = PsumRot(rec, ps_ab)
    y_rot = PsumRot(rec, ps_y)

    fslot_free = [None] * 6
    fslot_i = [0]

    def fstrip(src_ap, width, cast=False):
        sl = fslot_i[0] % 6
        fslot_i[0] += 1
        q = "gpsimd" if cast else "sync"
        sem = "GY" if cast else "SY"
        if fslot_free[sl] is not None:
            W(q, "PE", fslot_free[sl])
        ev = S(q, sem, lambda eng, sl=sl, src_ap=src_ap, width=width: eng.dma_start(
            out=fring[sl][:, 0:width], in_=src_ap), 16)
        return sl, ev, sem

    dw_ev = [None, None]
    for p in range(2):
        rec.barrier(("sync", "gpsimd"), ("DVE", "ACT", "PE"))
        ev_x = []
        for do in range(NCD):
            if do >= 2:
                W("sync", "DVE", ev_x[do - 2])
            evx = S("sync", "SY", lambda eng, do=do, p=p: eng.dma_start(
                out=scr[do % 2][:], in_=xo[do * 128:(do + 1) * 128, p * 1024:(p + 1) * 1024]), 16)
            W("vector", "SY", evx + 16)
            evy = S("vector", "DVE", lambda eng, do=do: eng.tensor_scalar(
                y_sb[:, do * 1024:(do + 1) * 1024], scr[do % 2][:],
                s_bo[:, do:do + 1], None, op0=ALU.add))
            ev_x.append(evy)
        def ytap(si, p=p):
            if DEBUG:
                W("gpsimd", "DVE", rec.cnt["DVE"])
                tev = S("gpsimd", "GY", lambda eng, si=si, p=p: eng.dma_start(
                    out=d_stage[p, si], in_=y_sb[:, 15 * 1024:16 * 1024]), 16)
                W("vector", "GY", tev)
        if DEBUG:
            W("gpsimd", "DVE", rec.cnt["DVE"])
            tev0 = S("gpsimd", "GY", lambda eng, p=p: eng.dma_start(out=d_scr[p, 0], in_=scr[0][:]), 16)
            tev0 = S("gpsimd", "GY", lambda eng, p=p: eng.dma_start(out=d_scr[p, 1], in_=scr[1][:]), 16)
            if DBGV in (1, 3):
                W("vector", "GY", tev0)
                W("scalar", "GY", tev0)
            if DBGV in (2, 3):
                W("sync", "GY", tev0)
        ytap(0)
        poh = ExitStack()
        o_half = poh.enter_context(nc.sbuf_tensor(f"o_half{p}", [128, 8 * 1024], BF16))
        for hp in range(2):
            W("sync", "GY", ev_o_spill_all + 16)
            W("sync", "PE", rec.cnt["PE"])
            ev_oh = None
            for hh in range(8):
                ev_oh = S("sync", "SY", lambda eng, hh=hh, hp=hp, p=p: eng.dma_start(
                    out=o_half[:, hh * 1024:(hh + 1) * 1024],
                    in_=osc[hp * 8 + hh][:, p * 1024:(p + 1) * 1024]), 16)
            flush("sync")
            for do in range(NCD):
                sl, evw, sem = fstrip(wo[do, hp], 1024, cast=True)
                for t in range(2):
                    ridx, reg = y_rot.acquire()
                    W("tensor", sem, evw + 16)
                    W("tensor", "SY", ev_oh + 16)
                    for c in range(8):
                        ev = S("tensor", "PE", lambda eng, sl=sl, c=c, t=t, reg=reg: eng.matmul(
                            reg[:], fring[sl][:, c * 128:(c + 1) * 128],
                            o_half[:, c * 1024 + t * 512: c * 1024 + t * 512 + 512],
                            start=(c == 0), stop=(c == 7)))
                    W("vector", "PE", ev)
                    W("vector", "DVE", ev_x[do])
                    evy = S("vector", "DVE", lambda eng, do=do, t=t, reg=reg: eng.tensor_add(
                        y_sb[:, do * 1024 + t * 512: do * 1024 + t * 512 + 512],
                        reg[:],
                        y_sb[:, do * 1024 + t * 512: do * 1024 + t * 512 + 512]))
                    y_rot.release(ridx, "DVE", evy)
                fslot_free[sl] = rec.cnt["PE"]
            flush("gpsimd")
        ytap(1)
        poh.close()
        ph = ExitStack()
        h_sb = ph.enter_context(nc.sbuf_tensor(f"h_sb{p}", [128, ICF * 1024], BF16))
        h_free_ev = [None]
        last_h_ev = {0: None, 1: None}
        W("scalar", "DVE", ev_x[-1])
        for e in range(NE + 2):
            shared = e >= NE
            sh_half = e - NE
            dpi = e % 2
            if not shared:
                W("tensor", "DVE", ev_dw_done)
                if dw_ev[0] is not None:
                    W("tensor", "DVE", dw_ev[0])
                if dw_ev[1] is not None:
                    W("tensor", "DVE", dw_ev[1])
                for t in range(2):
                    ev = S("tensor", "PE", lambda eng, e=e, t=t, p=p: eng.matmul(
                        ps_dwb[:, t * 512:(t + 1) * 512], s_sel[:, e * 128:(e + 1) * 128],
                        dw_em[:, p * 1024 + t * 512: p * 1024 + t * 512 + 512],
                        start=True, stop=True))
                W("vector", "PE", ev)
                dw_ev[dpi] = S("vector", "DVE", lambda eng, dpi=dpi: eng.tensor_copy(
                    dwb[dpi][:], ps_dwb[:]))
            ev_h_all = None
            for ci in range(ICF):
                if shared:
                    si = sh_half * ICF + ci
                    sl1, ev1, sem1 = fstrip(sw1[si], 2048)
                    sl3, ev3, sem3 = fstrip(sw3[si], 2048)
                else:
                    sl1, ev1, sem1 = fstrip(w1[e, ci], 2048)
                    sl3, ev3, sem3 = fstrip(w3[e, ci], 2048)
                for t in range(2):
                    aidx, areg = ab_rot.acquire()
                    W("tensor", sem1, ev1 + 16)
                    for c in range(NCD):
                        eva = S("tensor", "PE", lambda eng, sl1=sl1, c=c, t=t, p=p, areg=areg: eng.matmul(
                            areg[:], fring[sl1][:, c * 128:(c + 1) * 128],
                            xn_own[:, c * TOK + p * 1024 + t * 512: c * TOK + p * 1024 + t * 512 + 512],
                            start=(c == 0), stop=(c == NCD - 1)))
                    bidx, breg = ab_rot.acquire()
                    W("tensor", sem3, ev3 + 16)
                    for c in range(NCD):
                        evb = S("tensor", "PE", lambda eng, sl3=sl3, c=c, t=t, p=p, breg=breg: eng.matmul(
                            breg[:], fring[sl3][:, c * 128:(c + 1) * 128],
                            xn_own[:, c * TOK + p * 1024 + t * 512: c * TOK + p * 1024 + t * 512 + 512],
                            start=(c == 0), stop=(c == NCD - 1)))
                    W("scalar", "PE", eva)
                    if last_h_ev[t] is not None:
                        W("scalar", "DVE", last_h_ev[t])
                    evs = S("scalar", "ACT", lambda eng, t=t, areg=areg: eng.activation(
                        scr[0][:, t * 512:(t + 1) * 512], areg[:], AF.Silu))
                    ab_rot.release(aidx, "ACT", evs)
                    W("vector", "PE", evb)
                    if h_free_ev[0] is not None:
                        W("vector", "PE", h_free_ev[0])
                    if shared:
                        W("vector", "ACT", evs)
                        evh = S("vector", "DVE", lambda eng, ci=ci, t=t, breg=breg: eng.tensor_mul(
                            h_sb[:, ci * 1024 + t * 512: ci * 1024 + t * 512 + 512],
                            breg[:], scr[0][:, t * 512:(t + 1) * 512]))
                        ab_rot.release(bidx, "DVE", evh)
                    else:
                        W("vector", "DVE", dw_ev[dpi])
                        evt2 = S("vector", "DVE", lambda eng, t=t, breg=breg, dpi=dpi: eng.tensor_mul(
                            scr[1][:, t * 512:(t + 1) * 512], breg[:],
                            dwb[dpi][:, t * 512:(t + 1) * 512]))
                        ab_rot.release(bidx, "DVE", evt2)
                        W("vector", "ACT", evs)
                        evh = S("vector", "DVE", lambda eng, ci=ci, t=t: eng.tensor_mul(
                            h_sb[:, ci * 1024 + t * 512: ci * 1024 + t * 512 + 512],
                            scr[0][:, t * 512:(t + 1) * 512],
                            scr[1][:, t * 512:(t + 1) * 512]))
                    last_h_ev[t] = evh
                    ev_h_all = evh
                fslot_free[sl1] = rec.cnt["PE"]
                fslot_free[sl3] = rec.cnt["PE"]
            for do in range(NCD):
                if shared:
                    sl2, ev2, sem2 = fstrip(sw2[do, sh_half], ICF * 128)
                else:
                    sl2, ev2, sem2 = fstrip(w2[e, do], ICF * 128)
                for t in range(2):
                    ridx, reg = y_rot.acquire()
                    W("tensor", sem2, ev2 + 16)
                    W("tensor", "DVE", ev_h_all)
                    for ci in range(ICF):
                        ev = S("tensor", "PE", lambda eng, sl2=sl2, ci=ci, t=t, reg=reg: eng.matmul(
                            reg[:], fring[sl2][:, ci * 128:(ci + 1) * 128],
                            h_sb[:, ci * 1024 + t * 512: ci * 1024 + t * 512 + 512],
                            start=(ci == 0), stop=(ci == ICF - 1)))
                    W("vector", "PE", ev)
                    evy = S("vector", "DVE", lambda eng, do=do, t=t, reg=reg: eng.tensor_add(
                        y_sb[:, do * 1024 + t * 512: do * 1024 + t * 512 + 512],
                        reg[:],
                        y_sb[:, do * 1024 + t * 512: do * 1024 + t * 512 + 512]))
                    y_rot.release(ridx, "DVE", evy)
                fslot_free[sl2] = rec.cnt["PE"]
            h_free_ev[0] = rec.cnt["PE"]
            if e in (0, 7, 15, 16, 17):
                ytap({0: 2, 7: 3, 15: 4, 16: 5, 17: 6}[e])
        flush("sync")
        ph.close()
        if DEBUG:
            W("gpsimd", "DVE", rec.cnt["DVE"])
            S("gpsimd", "GY", lambda eng, p=p: eng.dma_start(
                out=d_ysb[p], in_=y_sb[:, 15 * 1024:16 * 1024]), 16)
        W("scalar", "DVE", rec.cnt["DVE"])
        for do in range(NCD):
            S("scalar", "SD", lambda eng, do=do, p=p: eng.dma_start(
                out=yT[do * 128:(do + 1) * 128, p * 1024:(p + 1) * 1024],
                in_=y_sb[:, do * 1024:(do + 1) * 1024]), 16)
        if p == 0:
            W("vector", "SD", rec.cnt["SD"])
    W("scalar", "SD", rec.cnt["SD"])
    pf.close()
    pfp.close()

    rec.run(fin_waits=[(p, rec.cnt[p]) for p in ("SY", "GY", "SD", "PE", "DVE", "ACT")],
            proto=PROTO)
    return nc


_cache = {}


def _host_prep(inputs):
    f32 = np.float32
    x = np.asarray(inputs["L0_attn_x"], dtype=f32)
    w_qkv = np.asarray(inputs["L0_attn_w_qkv"], dtype=f32)
    b_qkv = np.asarray(inputs["L0_attn_b_qkv"], dtype=f32)
    w_out = np.asarray(inputs["L0_attn_w_out"], dtype=f32)
    b_out = np.asarray(inputs["L0_attn_b_out"], dtype=f32)
    gate_w = np.asarray(inputs["L0_ffn_gate_w"], dtype=f32)
    w1 = np.asarray(inputs["L0_ffn_w1"])
    w2 = np.asarray(inputs["L0_ffn_w2"])
    w3 = np.asarray(inputs["L0_ffn_w3"])
    sw1 = np.asarray(inputs["L0_ffn_sw1"])
    sw2 = np.asarray(inputs["L0_ffn_sw2"])
    sw3 = np.asarray(inputs["L0_ffn_sw3"])

    shared = {
        "wq": np.ascontiguousarray(
            w_qkv.reshape(48, 128, 16, 128).transpose(0, 3, 2, 1)).reshape(48, 128, 2048),
        "wo": np.ascontiguousarray(
            w_out.reshape(16, 128, 16, 128).transpose(0, 3, 2, 1)
            .reshape(16, 128, 2, 8, 128).transpose(0, 2, 1, 3, 4)).reshape(16, 2, 128, 1024),
        "gw": np.ascontiguousarray(
            gate_w.reshape(16, 16, 128).transpose(2, 1, 0).reshape(128, 256)),
        "w1": np.ascontiguousarray(
            w1.reshape(16, 11, 128, 16, 128).transpose(0, 1, 4, 3, 2)).reshape(16, 11, 128, 2048),
        "w3": np.ascontiguousarray(
            w3.reshape(16, 11, 128, 16, 128).transpose(0, 1, 4, 3, 2)).reshape(16, 11, 128, 2048),
        "w2": np.ascontiguousarray(
            w2.reshape(16, 16, 128, 11, 128).transpose(0, 1, 4, 3, 2)).reshape(16, 16, 128, 1408),
        "sw1": np.ascontiguousarray(
            sw1.reshape(22, 128, 16, 128).transpose(0, 3, 2, 1)).reshape(22, 128, 2048),
        "sw3": np.ascontiguousarray(
            sw3.reshape(22, 128, 16, 128).transpose(0, 3, 2, 1)).reshape(22, 128, 2048),
        "sw2": np.ascontiguousarray(
            sw2.reshape(16, 128, 22, 128).transpose(0, 3, 2, 1)
            .reshape(16, 128, 2, 11, 128).transpose(0, 2, 1, 3, 4)).reshape(16, 2, 128, ICF * 128),
        "onesr": np.ones((1, 128), dtype=np.float32),
        "fpad": np.zeros((128, 16), dtype=np.float32),
    }
    sel = np.zeros((16, 2048), dtype=BF)
    for e in range(16):
        sel[e, e * 128:(e + 1) * 128] = 1
    shared["sel"] = sel
    cbf = np.zeros((128, 576), dtype=np.float32)
    cbf[:, 0:48] = np.ascontiguousarray(b_qkv.reshape(48, 128).T)
    cbf[:, 48:64] = np.ascontiguousarray(b_out.reshape(16, 128).T)
    shared["cblobf"] = cbf
    ident = np.eye(128, dtype=BF)
    dmask = np.where(np.arange(128)[None, :] <= np.arange(128)[:, None],
                     np.float32(0.0), np.float32(NEG)).astype(BF)
    onesc = np.ones((128, 4), dtype=BF)
    cbb = {}
    for hh, cmv in ((0, np.float32(NEG)), (1, np.float32(0.0))):
        blob = np.zeros((128, 1152), dtype=BF)
        blob[:, 0:128] = ident
        blob[:, 128:256] = dmask
        blob[:, 256:384] = np.full((128, 128), cmv, dtype=np.float32).astype(BF)
        blob[:, 384:388] = onesc
        cbb[hh] = blob

    in_maps = []
    for c in range(8):
        b, h = c // 2, c % 2
        xt = x[b].reshape(32, 128, D)
        m = dict(shared)
        m["xo"] = np.ascontiguousarray(xt[h::2].reshape(TOK, D).T)
        m["xp"] = np.ascontiguousarray(xt[1 - h::2].reshape(TOK, D).T)
        m["cblobb"] = cbb[h]
        in_maps.append(m)
    return in_maps


def kernel(**inputs):
    in_maps = _host_prep(inputs)
    if "nc" not in _cache:
        _cache["nc"] = build_program()
    nc = _cache["nc"]
    res = run_bass_kernel_spmd(nc, in_maps, core_ids=list(range(8)))
    out = np.empty((4, 4096, 2048), dtype=np.float32)
    ov = out.reshape(4, 32, 128, 2048)
    for c in range(8):
        b, h = c // 2, c % 2
        yt = np.asarray(res.results[c]["yT"], dtype=np.float32)
        ov[b, h::2] = yt.T.reshape(16, 128, 2048)
    return out
